# version 08
# baseline (speedup 1.0000x reference)
import math
import numpy as np
import ml_dtypes
import concourse.bass as bass
import concourse.mybir as mybir
from concourse.bass_utils import run_bass_kernel_spmd

F32 = mybir.dt.float32
BF16 = mybir.dt.bfloat16
AF = mybir.ActivationFunctionType
ALU = mybir.AluOpType
AX = mybir.AxisListType

P = 128
S = 2048
D = 2048
NT = S // P
ND = D // P
H = 16
HID = 8192

ENGINES = ("sync", "gpsimd", "tensor", "vector", "scalar")
CH = {"sync": "cs", "gpsimd": "cg", "tensor": "ct", "vector": "cv", "scalar": "cb"}


class Em:

    def __init__(self, which, engines, sems, paranoid=False):
        self.which = which
        self.eng = engines
        self.sems = sems
        self.c = {ch: 0 for ch in sems}
        self.paranoid = paranoid

    def _emit(self, en, fn, k):
        ch = CH[en]
        if self.paranoid and self.which == en:
            e = self.eng[en]
            for c2, val in self.c.items():
                if val > 0:
                    e.wait_ge(self.sems[c2], val)
        if self.which == en:
            ins = fn(self.eng[en])
            ins.then_inc(self.sems[ch], k)
        self.c[ch] += k

    def t(self, fn):
        self._emit("tensor", fn, 1)

    def v(self, fn):
        self._emit("vector", fn, 1)

    def s(self, fn):
        self._emit("scalar", fn, 1)

    def dma(self, out, in_, q="sync"):
        self._emit(q, lambda e: e.dma_start(out=out, in_=in_), 16)

    def fence(self, *engines):
        snap = dict(self.c)
        for en in engines:
            if self.which == en:
                e = self.eng[en]
                for ch, val in snap.items():
                    if val > 0 and ch != CH[en]:
                        e.wait_ge(self.sems[ch], val)

    def fence_all(self):
        self.fence(*ENGINES)


def build(dbg=False, stop=None, paranoid=False):
    nc = bass.Bass("TRN2", target_bir_lowering=False)

    def din(name, shape, dt=F32):
        return nc.dram_tensor(name, list(shape), dt, kind="ExternalInput").ap()

    x_in = din("x", (S, D))
    wqk = din("wqk", (D, 3072))
    wkvd = din("wkvd", (D, 512))
    wkvu = din("wkvu", (512, 3072))
    wproj = din("wproj", (D, D))
    gain = din("gain", (1, H))
    fw = din("fw", (D, D))
    fb = din("fb", (1, D))
    wq1 = din("wq1", (D, D))
    wk1 = din("wk1", (D, D))
    wv1 = din("wv1", (D, D))
    wo1 = din("wo1", (D, D))
    bq1 = din("bq1", (1, D))
    bk1 = din("bk1", (1, D))
    bv1 = din("bv1", (1, D))
    bo1 = din("bo1", (1, D))
    w1t = din("w1t", (D, HID))
    w3t = din("w3t", (D, HID))
    w2k = din("w2k", (P, 64 * D))
    identf = din("identf", (P, P), BF16)
    h128f = din("h128f", (P, P), BF16)
    masksf = din("masksf", (P, 2048), BF16)
    cosf = din("cosf", (P, 512))
    sinf = din("sinf", (P, 512))

    out = nc.dram_tensor("out", [S, D], F32, kind="ExternalOutput").ap()

    def scr(name, shape, dt):
        kind = "ExternalOutput" if dbg else "Internal"
        return nc.dram_tensor(name, list(shape), dt, kind=kind).ap()

    adbg = scr("adbg", (P, ND * S), BF16)
    qT_hbm = scr("qT_hbm", (H, P, S), BF16)
    kr_hbm = scr("kr_hbm", (S, 1024), BF16)
    v_hbm = scr("v_hbm", (S, D), BF16)
    yT_hbm = scr("yT_hbm", (H, P, S), BF16)
    acc_hbm = scr("acc_hbm", (S, D), F32)
    w2p = scr("w2p", (HID, D), BF16)

    A_ = nc.alloc_sbuf_tensor("arenaA", [P, ND * S], BF16)
    B_ = nc.alloc_sbuf_tensor("arenaB", [P, ND * S], BF16)
    W_ = nc.alloc_sbuf_tensor("arenaW", [P, 12288], BF16)
    SB_ = nc.alloc_sbuf_tensor("sb1", [P, 4096], BF16)
    XF_ = nc.alloc_sbuf_tensor("xf", [P, 2048], F32)
    ST_ = nc.alloc_sbuf_tensor("stt", [P, 6528], BF16)
    id_ = nc.alloc_sbuf_tensor("ident", [P, P], BF16)
    h128_ = nc.alloc_sbuf_tensor("h128", [P, P], BF16)
    mk_ = nc.alloc_sbuf_tensor("masks", [P, 2048], BF16)
    cs_ = nc.alloc_sbuf_tensor("cosb", [P, 512], F32)
    sn_ = nc.alloc_sbuf_tensor("sinb", [P, 512], F32)
    oc_ = nc.alloc_sbuf_tensor("onesc", [P, 1], BF16)
    or_ = nc.alloc_sbuf_tensor("onesr", [1, P], BF16)

    A = A_.ap()
    B = B_.ap()
    Wb = W_.ap()
    sb1 = SB_.ap()
    xf = XF_.ap()
    STb = ST_.ap()
    STf = STb.bitcast(F32)
    ident = id_.ap()
    h128 = h128_.ap()
    masks = mk_.ap()
    cosT = cs_.ap()
    sinT = sn_.ap()
    onesc = oc_.ap()
    onesr = or_.ap()

    Af32 = A.bitcast(F32)
    Bf32 = B.bitcast(F32)
    A3 = A.rearrange("p (c f) -> p c f", f=S)
    B3 = B.rearrange("p (c f) -> p c f", f=S)

    frow = STf[0:1, 160:2208]
    brow = STb[0:1, 4416:6464]

    PS_ = nc.alloc_psum_tensor("ps", [P, 4096], F32)
    PS = PS_.ap()

    sems = {}
    ctxs = []
    for ch in ("cs", "cg", "ct", "cv", "cb"):
        cm = nc.semaphore(ch)
        sems[ch] = cm.__enter__()
        ctxs.append(cm)
    blkcm = nc.Block()
    blk = blkcm.__enter__()

    engines = {"sync": nc.sync, "gpsimd": nc.gpsimd, "tensor": nc.tensor,
               "vector": nc.vector, "scalar": nc.scalar}

    RSQ = 1.0 / math.sqrt(128.0)

    def emit(em: Em):
        em.dma(ident, identf)
        em.dma(h128, h128f)
        em.dma(masks, masksf)
        em.dma(cosT, cosf)
        em.dma(sinT, sinf)
        em.dma(STf[0:1, 48:64], gain)
        em.fence("vector")
        em.v(lambda e: e.memset(onesc, 1.0))
        em.v(lambda e: e.memset(onesr, 1.0))
        em.v(lambda e: e.tensor_scalar(STf[0:1, 48:64], STf[0:1, 48:64],
                                       RSQ, None, ALU.mult))
        em.v(lambda e: e.tensor_scalar(STb[0:1, 4400:4416], STf[0:1, 48:64],
                                       1.0, None, ALU.mult))
        em.v(lambda e: e.tensor_tensor(STf[0:1, 100:116], STf[0:1, 48:64],
                                       STb[0:1, 4400:4416], ALU.subtract))
        em.v(lambda e: e.tensor_scalar(STb[0:1, 4384:4400], STf[0:1, 100:116],
                                       1.0, None, ALU.mult))
        em.fence("tensor")
        em.t(lambda e: e.matmul(PS[:, 0:16], onesr, STb[0:1, 4400:4416],
                                start=True, stop=False))
        em.t(lambda e: e.matmul(PS[:, 0:16], onesr, STb[0:1, 4384:4400],
                                start=False, stop=True))
        em.fence("scalar")
        em.s(lambda e: e.copy(STf[:, 120:136], PS[:, 0:16]))
        em.fence_all()

        def trans4(pairs):
            em.fence("tensor")
            for k, (_, src) in enumerate(pairs):
                em.t(lambda e, k=k, src=src: e.matmul(
                    PS[:, 3584 + k * 128:3712 + k * 128], src, ident,
                    start=True, stop=True))
            em.fence("scalar")
            for k, (dst, _) in enumerate(pairs):
                em.s(lambda e, k=k, dst=dst: e.copy(
                    dst, PS[:, 3584 + k * 128:3712 + k * 128]))

        def stage_w(dram, col0, width, nchunk=16):
            W3 = Wb.rearrange("p (c f) -> p c f", f=width)
            em.fence("gpsimd")
            for i in range(nchunk):
                em.dma(W3[:, i, :], dram[i * P:(i + 1) * P, col0:col0 + width],
                       q="gpsimd")
            return W3

        def ln_pass(src):
            for t in range(NT):
                em.fence("sync")
                em.dma(xf, src[t * P:(t + 1) * P, :])
                em.fence("vector", "tensor")
                x3 = xf.rearrange("p (a f) -> p a f", a=1)
                em.v(lambda e, x3=x3: e.reduce_sum(STf[:, 0:1], x3, axis=AX.X))
                em.v(lambda e: e.tensor_tensor(PS[:, 0:2048], xf, xf, ALU.mult))
                em.v(lambda e: e.reduce_sum(
                    STf[:, 1:2],
                    PS[:, 0:2048].rearrange("p (a f) -> p a f", a=1), axis=AX.X))
                em.v(lambda e: e.tensor_scalar(STf[:, 2:3], STf[:, 0:1],
                                               1.0 / D, None, ALU.mult))
                em.v(lambda e: e.tensor_tensor(STf[:, 3:4], STf[:, 2:3],
                                               STf[:, 2:3], ALU.mult))
                em.v(lambda e: e.tensor_scalar(STf[:, 4:5], STf[:, 1:2],
                                               1.0 / D, None, ALU.mult))
                em.v(lambda e: e.tensor_tensor(STf[:, 4:5], STf[:, 4:5],
                                               STf[:, 3:4], ALU.subtract))
                em.v(lambda e: e.tensor_scalar(STf[:, 4:5], STf[:, 4:5],
                                               1.0, 1e-5, ALU.mult, ALU.add))
                em.fence("scalar")
                em.s(lambda e: e.activation(STf[:, 5:6], STf[:, 4:5], AF.Sqrt))
                em.fence("vector")
                em.v(lambda e: e.reciprocal(STf[:, 6:7], STf[:, 5:6]))
                em.v(lambda e: e.tensor_scalar(
                    sb1[:, 0:2048], xf, STf[:, 2:3], STf[:, 6:7],
                    ALU.subtract, ALU.mult))
                for g in range(4):
                    trans4([(A3[:, g * 4 + k, t * P:(t + 1) * P],
                             sb1[:, (g * 4 + k) * P:(g * 4 + k + 1) * P])
                            for k in range(4)])

        def rms_rope(ps_off, width, nseg, do_rope, rope_off, t):
            src = PS[:, ps_off:ps_off + width]
            dst = PS[:, ps_off + width:ps_off + 2 * width]
            sxf = xf[:, 1024:1024 + width]
            em.v(lambda e: e.tensor_scalar(sxf, src, 1.0, None, ALU.mult))
            em.v(lambda e: e.tensor_tensor(dst, sxf, sxf, ALU.mult))
            em.v(lambda e: e.reduce_sum(
                STf[:, 8:8 + nseg],
                dst.rearrange("p (s d) -> p s d", d=64), axis=AX.X))
            em.v(lambda e: e.tensor_scalar(STf[:, 8:8 + nseg], STf[:, 8:8 + nseg],
                                           1.0 / 64, 1e-6, ALU.mult, ALU.add))
            em.fence("scalar")
            em.s(lambda e: e.activation(STf[:, 40:40 + nseg],
                                        STf[:, 8:8 + nseg], AF.Sqrt))
            em.fence("vector")
            em.v(lambda e: e.reciprocal(STf[:, 72:72 + nseg],
                                        STf[:, 40:40 + nseg]))
            st3 = STf[:, 72:72 + nseg].rearrange(
                "p (s o) -> p s o", o=1).broadcast_to((P, nseg, 64))
            em.v(lambda e: e.tensor_tensor(
                dst.rearrange("p (s d) -> p s d", d=64),
                sxf.rearrange("p (s d) -> p s d", d=64), st3, ALU.mult))
            if do_rope:
                nh = width // 128 if rope_off else width // 64
                span = 128 if rope_off else 64
                r3 = dst.rearrange("p (h q) -> p h q", q=span)
                x1 = r3[:, :, rope_off:rope_off + 32]
                x2 = r3[:, :, rope_off + 32:rope_off + 64]
                cc = cosT[:, t * 32:(t + 1) * 32].rearrange(
                    "p (o f) -> p o f", o=1).broadcast_to((P, nh, 32))
                ss = sinT[:, t * 32:(t + 1) * 32].rearrange(
                    "p (o f) -> p o f", o=1).broadcast_to((P, nh, 32))
                q1 = xf[:, 0:nh * 32].rearrange("p (h q) -> p h q", q=32)
                q2 = xf[:, 256:256 + nh * 32].rearrange("p (h q) -> p h q", q=32)
                q3 = xf[:, 512:512 + nh * 32].rearrange("p (h q) -> p h q", q=32)
                em.v(lambda e: e.tensor_tensor(q1, x1, cc, ALU.mult))
                em.v(lambda e: e.tensor_tensor(q2, x2, ss, ALU.mult))
                em.v(lambda e: e.tensor_tensor(q3, x1, ss, ALU.mult))
                em.v(lambda e: e.tensor_tensor(x1, q1, q2, ALU.add))
                em.v(lambda e: e.tensor_tensor(q1, x2, cc, ALU.mult))
                em.v(lambda e: e.tensor_tensor(x2, q1, q3, ALU.subtract))
            return dst

        w2k3 = w2k.rearrange("p (i d) -> p i d", d=D)
        w2p3 = w2p.rearrange("(j p) d -> p j d", p=P)
        for b in range(16):
            em.fence("sync")
            em.dma(Af32[:, 0:8192].rearrange("p (i d) -> p i d", d=128),
                   w2k3[:, :, b * P:(b + 1) * P])
            em.fence("vector")
            cur, oth = Af32, Bf32
            h = 1
            while h < 64:
                for g in range(64 // (2 * h)):
                    a0 = g * 2 * h * 128
                    w = h * 128
                    em.v(lambda e, a0=a0, w=w, cur=cur, oth=oth: e.tensor_tensor(
                        oth[:, a0:a0 + w], cur[:, a0:a0 + w],
                        cur[:, a0 + w:a0 + 2 * w], ALU.add))
                    em.v(lambda e, a0=a0, w=w, cur=cur, oth=oth: e.tensor_tensor(
                        oth[:, a0 + w:a0 + 2 * w], cur[:, a0:a0 + w],
                        cur[:, a0 + w:a0 + 2 * w], ALU.subtract))
                cur, oth = oth, cur
                h *= 2
            em.fence("vector")
            em.v(lambda e, cur=cur: e.tensor_scalar(
                Wb[:, 0:8192], cur[:, 0:8192], 1.0 / math.sqrt(HID),
                None, ALU.mult))
            for jq in range(16):
                em.fence("tensor")
                for k in range(4):
                    em.t(lambda e, jq=jq, k=k: e.matmul(
                        PS[:, k * 128:(k + 1) * 128], h128,
                        Wb[:, (jq * 4 + k) * 128:(jq * 4 + k + 1) * 128],
                        start=True, stop=True))
                em.fence("scalar")
                for k in range(4):
                    em.s(lambda e, k=k: e.copy(sb1[:, k * 128:(k + 1) * 128],
                                               PS[:, k * 128:(k + 1) * 128]))
                em.fence("sync")
                em.dma(w2p3[:, jq * 4:jq * 4 + 4, b * P:(b + 1) * P],
                       sb1[:, 0:512].rearrange("p (j f) -> p j f", f=128))
        em.fence_all()

        ln_pass(x_in)
        em.fence_all()
        if stop == "ln0":
            em.dma(adbg, A)
            em.fence_all()
            return
        if stop == "stw":
            stage_w(wqk, 0, 512)
            em.fence_all()
            em.dma(adbg[:, 0:8192], Wb[:, 0:8192])
            em.fence_all()
            return

        for c in range(ND):
            em.fence("sync")
            em.dma(xf, fw[c * P:(c + 1) * P, :])
            em.fence("vector")
            em.v(lambda e: e.scalar_tensor_tensor(PS[:, 0:2048], xf, -1.0, xf,
                                                  ALU.mult, ALU.max))
            em.v(lambda e: e.reduce_sum(
                STf[:, 0:1],
                PS[:, 0:2048].rearrange("p (a f) -> p a f", a=1), axis=AX.X))
            em.v(lambda e: e.tensor_scalar(STf[:, 1:2], STf[:, 0:1],
                                           0.7 / D, None, ALU.mult))
            em.v(lambda e: e.tensor_scalar(STf[:, 2:3], STf[:, 1:2],
                                           -1.0, None, ALU.mult))
            em.v(lambda e: e.tensor_scalar(sb1[:, 0:2048], xf,
                                           STf[:, 1:2], None, ALU.is_gt))
            em.v(lambda e: e.tensor_scalar(PS[:, 2048:4096], xf,
                                           STf[:, 2:3], None, ALU.is_lt))
            em.v(lambda e: e.tensor_tensor(sb1[:, 2048:4096], sb1[:, 0:2048],
                                           PS[:, 2048:4096], ALU.subtract))
            for g in range(4):
                trans4([(B3[:, g * 4 + k, c * P:(c + 1) * P],
                         sb1[:, 2048 + (g * 4 + k) * P:2048 + (g * 4 + k + 1) * P])
                        for k in range(4)])
        em.fence("sync")
        em.dma(frow, fb)
        em.fence("vector")
        em.v(lambda e: e.tensor_scalar(brow, frow, 1.0, None, ALU.mult))
        em.fence_all()
        for t in range(NT):
            em.fence("tensor", "sync", "gpsimd")
            em.dma(xf, x_in[t * P:(t + 1) * P, :], q="gpsimd")
            for j in range(4):
                for i in range(ND):
                    em.t(lambda e, i=i, j=j, t=t: e.matmul(
                        PS[:, j * 512:(j + 1) * 512], A3[:, i, t * P:(t + 1) * P],
                        B3[:, i, j * 512:(j + 1) * 512],
                        start=(i == 0), stop=False))
                em.t(lambda e, j=j: e.matmul(
                    PS[:, j * 512:(j + 1) * 512], onesr,
                    brow[0:1, j * 512:(j + 1) * 512], start=False, stop=True))
            em.fence("vector")
            em.v(lambda e: e.tensor_tensor(xf, PS[:, 0:2048], xf, ALU.add))
            em.fence("sync")
            em.dma(acc_hbm[t * P:(t + 1) * P, :], xf)
        em.fence_all()
        if stop == "ffn0":
            em.dma(adbg, B)
            em.fence_all()
            return

        for pc in range(6):
            W3 = stage_w(wqk, pc * 512, 512)
            em.fence_all()
            for t in range(NT):
                em.fence("tensor")
                for i in range(ND):
                    em.t(lambda e, i=i, t=t: e.matmul(
                        PS[:, 0:512], A3[:, i, t * P:(t + 1) * P],
                        W3[:, i, :], start=(i == 0), stop=(i == ND - 1)))
                em.fence("vector")
                if pc < 4:
                    rms_rope(0, 512, 8, True, 64, t)
                    em.v(lambda e: e.tensor_scalar(sb1[:, 0:512], PS[:, 512:1024],
                                                   1.0, None, ALU.mult))
                    trans4([(sb1[:, 1024 + k * 128:1152 + k * 128],
                             sb1[:, k * 128:(k + 1) * 128]) for k in range(4)])
                    em.fence("sync")
                    for k in range(4):
                        em.dma(qT_hbm[pc * 4 + k, :, t * P:(t + 1) * P],
                               sb1[:, 1024 + k * 128:1152 + k * 128])
                else:
                    rms_rope(0, 512, 8, True, 0, t)
                    em.v(lambda e: e.tensor_scalar(sb1[:, 0:512], PS[:, 512:1024],
                                                   1.0, None, ALU.mult))
                    em.fence("sync")
                    em.dma(kr_hbm[t * P:(t + 1) * P, (pc - 4) * 512:(pc - 3) * 512],
                           sb1[:, 0:512])
        em.fence_all()
        if stop == "qkrope":
            return

        W3 = stage_w(wkvd, 0, 512)
        em.fence_all()
        B3c = B.rearrange("p (c f) -> p c f", f=S)
        for t in range(NT):
            em.fence("tensor")
            for i in range(ND):
                em.t(lambda e, i=i, t=t: e.matmul(
                    PS[:, 0:512], A3[:, i, t * P:(t + 1) * P],
                    W3[:, i, :], start=(i == 0), stop=(i == ND - 1)))
            em.v(lambda e: e.tensor_scalar(xf[:, 1024:1536], PS[:, 0:512],
                                           1.0, None, ALU.mult))
            em.v(lambda e: e.tensor_tensor(PS[:, 512:1024], xf[:, 1024:1536],
                                           xf[:, 1024:1536], ALU.mult))
            em.v(lambda e: e.reduce_sum(
                STf[:, 8:9],
                PS[:, 512:1024].rearrange("p (a f) -> p a f", a=1), axis=AX.X))
            em.v(lambda e: e.tensor_scalar(STf[:, 8:9], STf[:, 8:9],
                                           1.0 / 512, 1e-6, ALU.mult, ALU.add))
            em.fence("scalar")
            em.s(lambda e: e.activation(STf[:, 9:10], STf[:, 8:9], AF.Sqrt))
            em.fence("vector")
            em.v(lambda e: e.reciprocal(STf[:, 10:11], STf[:, 9:10]))
            em.v(lambda e: e.tensor_scalar(sb1[:, 0:512], xf[:, 1024:1536],
                                           STf[:, 10:11], None, ALU.mult))
            trans4([(B3c[:, k, t * P:(t + 1) * P],
                     sb1[:, k * 128:(k + 1) * 128]) for k in range(4)])
        em.fence_all()

        W3u = stage_w(wkvu, 0, 3072, nchunk=4)
        em.fence_all()
        for t in range(NT):
            em.fence("tensor", "sync", "gpsimd")
            em.dma(sb1[:, 1024:2048], kr_hbm[t * P:(t + 1) * P, :], q="gpsimd")
            for nb in range(6):
                for i in range(4):
                    em.t(lambda e, i=i, nb=nb, t=t: e.matmul(
                        PS[:, nb * 512:(nb + 1) * 512],
                        B3c[:, i, t * P:(t + 1) * P],
                        W3u[:, i, nb * 512:(nb + 1) * 512],
                        start=(i == 0), stop=(i == 3)))
            em.fence("vector", "scalar")
            em.v(lambda e: e.tensor_scalar(xf[:, 1024:2048], PS[:, 0:1024],
                                           1.0, None, ALU.mult))
            em.v(lambda e: e.tensor_tensor(PS[:, 3072:4096], xf[:, 1024:2048],
                                           xf[:, 1024:2048], ALU.mult))
            em.v(lambda e: e.reduce_sum(
                STf[:, 8:24],
                PS[:, 3072:4096].rearrange("p (s d) -> p s d", d=64), axis=AX.X))
            em.v(lambda e: e.tensor_scalar(STf[:, 8:24], STf[:, 8:24],
                                           1.0 / 64, 1e-6, ALU.mult, ALU.add))
            em.fence("scalar")
            em.s(lambda e: e.activation(STf[:, 40:56], STf[:, 8:24], AF.Sqrt))
            em.fence("vector")
            em.v(lambda e: e.reciprocal(STf[:, 72:88], STf[:, 40:56]))
            st3 = STf[:, 72:88].rearrange("p (s o) -> p s o", o=1).broadcast_to((P, 16, 64))
            em.v(lambda e: e.tensor_tensor(
                sb1[:, 0:1024].rearrange("p (s d) -> p s d", d=64),
                xf[:, 1024:2048].rearrange("p (s d) -> p s d", d=64),
                st3, ALU.mult))
            kf3 = sb1[:, 2048:4096].rearrange("p (h q) -> p h q", q=128)
            em.v(lambda e: e.tensor_scalar(
                kf3[:, :, 0:64],
                sb1[:, 0:1024].rearrange("p (s d) -> p s d", d=64),
                1.0, None, ALU.mult))
            em.v(lambda e: e.tensor_scalar(
                kf3[:, :, 64:128],
                sb1[:, 1024:2048].rearrange("p (s d) -> p s d", d=64),
                1.0, None, ALU.mult))
            for g in range(4):
                trans4([(A3[:, g * 4 + k, t * P:(t + 1) * P],
                         sb1[:, 2048 + (g * 4 + k) * P:2048 + (g * 4 + k + 1) * P])
                        for k in range(4)])
            em.fence("vector")
            em.v(lambda e: e.tensor_scalar(sb1[:, 0:2048], PS[:, 1024:3072],
                                           1.0, None, ALU.mult))
            em.fence("sync")
            em.dma(v_hbm[t * P:(t + 1) * P, :], sb1[:, 0:2048])
        em.fence_all()

        v_hbm3 = v_hbm.rearrange("(u p) d -> p u d", p=P)

        def attention(kT3, layer):
            for h in range(H):
                em.fence("sync", "gpsimd")
                em.dma(Wb[:, 0:2048], qT_hbm[h])
                em.dma(Wb[:, 2048:4096].rearrange("p (u f) -> p u f", f=128),
                       v_hbm3[:, :, h * 128:(h + 1) * 128], q="gpsimd")
                em.fence_all()
                for qb in range(4):
                    nu = 4 * qb + 4
                    for u in range(nu):
                        em.t(lambda e, u=u, h=h, qb=qb: e.matmul(
                            PS[:, 0:512], kT3[:, h, u * P:(u + 1) * P],
                            Wb[:, qb * 512:(qb + 1) * 512],
                            start=True, stop=True))
                        em.fence("scalar")
                        if layer == 0:
                            em.s(lambda e, h=h: e.activation(
                                sb1[:, 0:512], PS[:, 0:512], AF.Exp,
                                scale=STf[:, 120 + h:121 + h]))
                        else:
                            em.s(lambda e: e.activation(
                                sb1[:, 0:512], PS[:, 0:512], AF.Exp, scale=RSQ))
                        j = u - 4 * qb
                        em.fence("vector")
                        if j >= 0:
                            em.v(lambda e, j=j: e.tensor_tensor(
                                sb1[:, 0:512], sb1[:, 0:512],
                                masks[:, j * 512:(j + 1) * 512], ALU.mult))
                        em.fence("tensor")
                        em.t(lambda e, u=u: e.matmul(
                            PS[:, 1024:1536], Wb[:, 2048 + u * 128:2176 + u * 128],
                            sb1[:, 0:512], start=(u == 0), stop=(u == nu - 1)))
                        em.t(lambda e, u=u: e.matmul(
                            PS[0:1, 1536:2048], onesc,
                            sb1[:, 0:512], start=(u == 0), stop=(u == nu - 1)))
                    em.fence("vector", "scalar")
                    em.v(lambda e: e.reciprocal(xf[0:1, 1024:1536],
                                                PS[0:1, 1536:2048]))
                    em.v(lambda e: e.tensor_scalar(sb1[0:1, 3072:3584],
                                                   xf[0:1, 1024:1536],
                                                   1.0, None, ALU.mult))
                    em.v(lambda e: e.tensor_tensor(sb1[0:1, 3584:4096],
                                                   xf[0:1, 1024:1536],
                                                   sb1[0:1, 3072:3584],
                                                   ALU.subtract))
                    em.s(lambda e: e.copy(xf[:, 0:512], PS[:, 1024:1536]))
                    em.fence("tensor")
                    em.t(lambda e: e.matmul(PS[:, 2048:2560], onesr,
                                            sb1[0:1, 3072:3584],
                                            start=True, stop=False))
                    em.t(lambda e: e.matmul(PS[:, 2048:2560], onesr,
                                            sb1[0:1, 3584:4096],
                                            start=False, stop=True))
                    em.fence("vector")
                    if layer == 0:
                        em.v(lambda e, h=h, qb=qb: e.tensor_tensor(
                            B3[:, h, qb * 512:(qb + 1) * 512],
                            xf[:, 0:512], PS[:, 2048:2560], ALU.mult))
                    else:
                        em.v(lambda e: e.tensor_tensor(
                            sb1[:, 1024:1536],
                            xf[:, 0:512], PS[:, 2048:2560], ALU.mult))
                        em.fence("sync")
                        em.dma(yT_hbm[h, :, qb * 512:(qb + 1) * 512],
                               sb1[:, 1024:1536])

        if stop == "kvup":
            em.dma(adbg, B)
            em.fence_all()
            return
        attention(A3, 0)
        em.fence_all()
        if stop == "attn0":
            return

        def proj_pass(yT3, wsrc, bias_row):
            for q in range(4):
                W3 = stage_w(wsrc, q * 512, 512)
                em.fence_all()
                for t in range(NT):
                    em.fence("tensor", "sync", "gpsimd")
                    em.dma(xf[:, 0:512],
                           acc_hbm[t * P:(t + 1) * P, q * 512:(q + 1) * 512],
                           q="gpsimd")
                    for i in range(ND):
                        em.t(lambda e, i=i, t=t, yT3=yT3, W3=W3: e.matmul(
                            PS[:, 0:512], yT3[:, i, t * P:(t + 1) * P],
                            W3[:, i, :], start=(i == 0), stop=(bias_row is None and i == ND - 1)))
                    if bias_row is not None:
                        em.t(lambda e, q=q: e.matmul(
                            PS[:, 0:512], onesr,
                            bias_row[0:1, q * 512:(q + 1) * 512],
                            start=False, stop=True))
                    em.fence("vector", "gpsimd")
                    em.v(lambda e: e.tensor_tensor(xf[:, 0:512], PS[:, 0:512],
                                                   xf[:, 0:512], ALU.add))
                    em.fence("sync")
                    em.dma(acc_hbm[t * P:(t + 1) * P, q * 512:(q + 1) * 512],
                           xf[:, 0:512])

        proj_pass(B3, wproj, None)
        em.fence_all()
        if stop == "proj0":
            return

        ln_pass(acc_hbm)
        em.fence_all()

        for wsrc, bsrc, kind in ((wq1, bq1, "q"), (wk1, bk1, "k"),
                                 (wv1, bv1, "v")):
            em.fence("sync")
            em.dma(frow, bsrc)
            em.fence("vector")
            em.v(lambda e: e.tensor_scalar(brow, frow, 1.0, None, ALU.mult))
            for q in range(4):
                W3 = stage_w(wsrc, q * 512, 512)
                em.fence_all()
                for t in range(NT):
                    em.fence("tensor")
                    for i in range(ND):
                        em.t(lambda e, i=i, t=t, W3=W3: e.matmul(
                            PS[:, 0:512], A3[:, i, t * P:(t + 1) * P],
                            W3[:, i, :], start=(i == 0), stop=False))
                    em.t(lambda e, q=q: e.matmul(
                        PS[:, 0:512], onesr, brow[0:1, q * 512:(q + 1) * 512],
                        start=False, stop=True))
                    em.fence("vector")
                    em.v(lambda e: e.tensor_scalar(sb1[:, 0:512], PS[:, 0:512],
                                                   1.0, None, ALU.mult))
                    if kind == "v":
                        em.fence("sync")
                        em.dma(v_hbm[t * P:(t + 1) * P, q * 512:(q + 1) * 512],
                               sb1[:, 0:512])
                    elif kind == "k":
                        trans4([(B3[:, q * 4 + k, t * P:(t + 1) * P],
                                 sb1[:, k * 128:(k + 1) * 128])
                                for k in range(4)])
                    else:
                        trans4([(sb1[:, 1024 + k * 128:1152 + k * 128],
                                 sb1[:, k * 128:(k + 1) * 128])
                                for k in range(4)])
                        em.fence("sync")
                        for k in range(4):
                            em.dma(qT_hbm[q * 4 + k, :, t * P:(t + 1) * P],
                                   sb1[:, 1024 + k * 128:1152 + k * 128])
        em.fence_all()

        attention(B3, 1)
        em.fence_all()
        if stop == "attn1":
            return

        em.fence("sync")
        for h in range(H):
            em.dma(B3[:, h, :], yT_hbm[h])
        em.fence("sync")
        em.dma(frow, bo1)
        em.fence("vector")
        em.v(lambda e: e.tensor_scalar(brow, frow, 1.0, None, ALU.mult))
        em.fence_all()
        proj_pass(B3, wo1, brow)
        em.fence_all()
        if stop == "proj1":
            return

        B3h = B.rearrange("p (c f) -> p c f", f=S)
        B3w3 = B[:, 16384:24576].rearrange("p (c f) -> p c f", f=512)
        for g in range(8):
            for sub in range(2):
                col0 = g * 1024 + sub * 512
                W3a = stage_w(w1t, col0, 512)
                em.fence("gpsimd")
                for i in range(ND):
                    em.dma(B3w3[:, i, :], w3t[i * P:(i + 1) * P, col0:col0 + 512],
                           q="gpsimd")
                em.fence_all()
                for t in range(NT):
                    em.fence("tensor")
                    for i in range(ND):
                        em.t(lambda e, i=i, t=t, W3a=W3a: e.matmul(
                            PS[:, 0:512], A3[:, i, t * P:(t + 1) * P],
                            W3a[:, i, :], start=(i == 0), stop=(i == ND - 1)))
                    for i in range(ND):
                        em.t(lambda e, i=i, t=t: e.matmul(
                            PS[:, 512:1024], A3[:, i, t * P:(t + 1) * P],
                            B3w3[:, i, :], start=(i == 0), stop=(i == ND - 1)))
                    em.fence("scalar")
                    em.s(lambda e: e.activation(xf[:, 0:512], PS[:, 0:512],
                                                AF.Silu))
                    em.fence("vector")
                    em.v(lambda e: e.tensor_tensor(sb1[:, 0:512], xf[:, 0:512],
                                                   PS[:, 512:1024], ALU.mult))
                    trans4([(B3h[:, sub * 4 + k, t * P:(t + 1) * P],
                             sb1[:, k * 128:(k + 1) * 128]) for k in range(4)])
            for dh in range(2):
                W3b = Wb.rearrange("p (c f) -> p c f", f=1024)
                em.fence("sync")
                for k in range(8):
                    em.dma(W3b[:, k, :],
                           w2p[(g * 8 + k) * P:(g * 8 + k + 1) * P,
                               dh * 1024:(dh + 1) * 1024])
                em.fence_all()
                for t in range(NT):
                    em.fence("tensor", "sync", "gpsimd")
                    em.dma(xf[:, 0:1024],
                           acc_hbm[t * P:(t + 1) * P, dh * 1024:(dh + 1) * 1024],
                           q="gpsimd")
                    for k in range(8):
                        for nb in range(2):
                            em.t(lambda e, k=k, nb=nb, t=t, W3b=W3b: e.matmul(
                                PS[:, nb * 512:(nb + 1) * 512],
                                B3h[:, k, t * P:(t + 1) * P],
                                W3b[:, k, nb * 512:(nb + 1) * 512],
                                start=(k == 0), stop=(k == 7)))
                    em.fence("vector", "gpsimd")
                    em.v(lambda e: e.tensor_tensor(xf[:, 0:1024], PS[:, 0:1024],
                                                   xf[:, 0:1024], ALU.add))
                    em.fence("sync")
                    dst = out if g == 7 else acc_hbm
                    em.dma(dst[t * P:(t + 1) * P, dh * 1024:(dh + 1) * 1024],
                           xf[:, 0:1024])
        em.fence_all()

    for name in ENGINES:
        getattr(blk, name)(lambda eng, name=name: emit(
            Em(name, engines, sems, paranoid=paranoid)))

    blkcm.__exit__(None, None, None)
    for cm in reversed(ctxs):
        cm.__exit__(None, None, None)
    return nc


def _had(n):
    h = np.array([[1.0]], dtype=np.float32)
    while h.shape[0] < n:
        h = np.block([[h, h], [h, -h]])
    return h


def _consts():
    bf = ml_dtypes.bfloat16
    identf = np.eye(P, dtype=np.float32).astype(bf)
    h128f = _had(P).astype(bf)
    masks = np.zeros((P, 4, 512), np.float32)
    r = np.arange(P)[:, None]
    c = np.arange(512)[None, :]
    for j in range(4):
        masks[:, j, :] = (c >= j * 128 + r).astype(np.float32)
    masksf = masks.reshape(P, 2048).astype(bf)
    inv = 10000.0 ** (-np.arange(0, 64, 2, dtype=np.float32) / 64.0)
    tpos = np.arange(S, dtype=np.float32)
    fr = np.outer(tpos, inv)
    cosf = np.cos(fr).reshape(NT, P, 32).transpose(1, 0, 2).reshape(P, 512)
    sinf = np.sin(fr).reshape(NT, P, 32).transpose(1, 0, 2).reshape(P, 512)
    return identf, h128f, masksf, cosf.astype(np.float32), sinf.astype(np.float32)


_NC = None


def kernel(L0_attn_x, L0_attn_w_q_krope, L0_attn_w_kv_down, L0_attn_w_kv_up,
           L0_attn_w_proj, L0_attn_q_gain, L0_ffn_weight, L0_ffn_bias,
           L1_attn_Wq, L1_attn_bq, L1_attn_Wk, L1_attn_bk, L1_attn_Wv,
           L1_attn_bv, L1_attn_Wo, L1_attn_bo, L1_ffn_W1, L1_ffn_W2,
           L1_ffn_W3):
    global _NC
    if _NC is None:
        _NC = build(paranoid=True)
    nc = _NC
    f32 = np.float32
    cc = np.ascontiguousarray
    identf, h128f, masksf, cosf, sinf = _consts()
    w2k = cc(np.asarray(L1_ffn_W2, f32).T.reshape(64, P, D)
             .transpose(1, 0, 2).reshape(P, 64 * D))
    shared = {
        "wqk": cc(np.asarray(L0_attn_w_q_krope, f32)),
        "wkvd": cc(np.asarray(L0_attn_w_kv_down, f32)),
        "wkvu": cc(np.asarray(L0_attn_w_kv_up, f32)),
        "wproj": cc(np.asarray(L0_attn_w_proj, f32)),
        "gain": cc(np.asarray(L0_attn_q_gain, f32).reshape(1, H)),
        "fw": cc(np.asarray(L0_ffn_weight, f32)),
        "fb": cc(np.asarray(L0_ffn_bias, f32).reshape(1, D)),
        "wq1": cc(np.asarray(L1_attn_Wq, f32).T),
        "wk1": cc(np.asarray(L1_attn_Wk, f32).T),
        "wv1": cc(np.asarray(L1_attn_Wv, f32).T),
        "wo1": cc(np.asarray(L1_attn_Wo, f32).T),
        "bq1": cc(np.asarray(L1_attn_bq, f32).reshape(1, D)),
        "bk1": cc(np.asarray(L1_attn_bk, f32).reshape(1, D)),
        "bv1": cc(np.asarray(L1_attn_bv, f32).reshape(1, D)),
        "bo1": cc(np.asarray(L1_attn_bo, f32).reshape(1, D)),
        "w1t": cc(np.asarray(L1_ffn_W1, f32).T),
        "w3t": cc(np.asarray(L1_ffn_W3, f32).T),
        "w2k": w2k,
        "identf": identf, "h128f": h128f, "masksf": masksf,
        "cosf": cosf, "sinf": sinf,
    }
    x = np.asarray(L0_attn_x, f32)
    in_maps = []
    for b in range(8):
        m = dict(shared)
        m["x"] = cc(x[b])
        in_maps.append(m)
    res = run_bass_kernel_spmd(nc, in_maps, core_ids=list(range(8)))
    return np.stack([r["out"] for r in res.results], axis=0).astype(np.float32)


# version 09 (final)
# speedup vs baseline: 2.1231x; 2.1231x over previous
import math
import numpy as np
import ml_dtypes
import concourse.bass as bass
import concourse.mybir as mybir
from concourse.bass_utils import run_bass_kernel_spmd

F32 = mybir.dt.float32
BF16 = mybir.dt.bfloat16
AF = mybir.ActivationFunctionType
ALU = mybir.AluOpType
AX = mybir.AxisListType

P = 128
S = 2048
D = 2048
NT = S // P
ND = D // P
H = 16
HID = 8192

ENGINES = ("sync", "gpsimd", "tensor", "vector", "scalar")
CH = {"sync": "cs", "gpsimd": "cg", "tensor": "ct", "vector": "cv", "scalar": "cb"}


class Em:

    def __init__(self, which, engines, sems, paranoid=False, selfwait=False):
        self.which = which
        self.eng = engines
        self.sems = sems
        self.c = {ch: 0 for ch in sems}
        self.paranoid = paranoid
        self.selfwait = selfwait

    def _emit(self, en, fn, k):
        ch = CH[en]
        if self.which == en:
            e = self.eng[en]
            if self.paranoid:
                for c2, val in self.c.items():
                    if val > 0:
                        e.wait_ge(self.sems[c2], val)
            elif self.selfwait and en in ("vector", "scalar") and self.c[ch] > 0:
                e.wait_ge(self.sems[ch], self.c[ch])
        if self.which == en:
            ins = fn(self.eng[en])
            ins.then_inc(self.sems[ch], k)
        self.c[ch] += k

    def t(self, fn):
        self._emit("tensor", fn, 1)

    def v(self, fn):
        self._emit("vector", fn, 1)

    def s(self, fn):
        self._emit("scalar", fn, 1)

    def dma(self, out, in_, q="sync"):
        self._emit(q, lambda e: e.dma_start(out=out, in_=in_), 16)

    def fence(self, *engines):
        snap = dict(self.c)
        for en in engines:
            if self.which == en:
                e = self.eng[en]
                for ch, val in snap.items():
                    if val > 0 and ch != CH[en]:
                        e.wait_ge(self.sems[ch], val)

    def fence_all(self):
        self.fence(*ENGINES)


def build(dbg=False, stop=None, paranoid=False, selfwait=True):
    nc = bass.Bass("TRN2", target_bir_lowering=False)

    def din(name, shape, dt=F32):
        return nc.dram_tensor(name, list(shape), dt, kind="ExternalInput").ap()

    x_in = din("x", (S, D))
    wqk = din("wqk", (D, 3072))
    wkvd = din("wkvd", (D, 512))
    wkvu = din("wkvu", (512, 3072))
    wproj = din("wproj", (D, D))
    gain = din("gain", (1, H))
    fw = din("fw", (D, D))
    fb = din("fb", (1, D))
    wq1 = din("wq1", (D, D))
    wk1 = din("wk1", (D, D))
    wv1 = din("wv1", (D, D))
    wo1 = din("wo1", (D, D))
    bq1 = din("bq1", (1, D))
    bk1 = din("bk1", (1, D))
    bv1 = din("bv1", (1, D))
    bo1 = din("bo1", (1, D))
    w1t = din("w1t", (D, HID))
    w3t = din("w3t", (D, HID))
    w2k = din("w2k", (P, 64 * D))
    identf = din("identf", (P, P), BF16)
    h128f = din("h128f", (P, P), BF16)
    masksf = din("masksf", (P, 2048), BF16)
    cosf = din("cosf", (P, 512))
    sinf = din("sinf", (P, 512))

    out = nc.dram_tensor("out", [S, D], F32, kind="ExternalOutput").ap()

    def scr(name, shape, dt):
        kind = "ExternalOutput" if dbg else "Internal"
        return nc.dram_tensor(name, list(shape), dt, kind=kind).ap()

    adbg = scr("adbg", (P, ND * S), BF16)
    qT_hbm = scr("qT_hbm", (H, P, S), BF16)
    kr_hbm = scr("kr_hbm", (S, 1024), BF16)
    v_hbm = scr("v_hbm", (S, D), BF16)
    yT_hbm = scr("yT_hbm", (H, P, S), BF16)
    acc_hbm = scr("acc_hbm", (S, D), F32)
    w2p = scr("w2p", (HID, D), BF16)

    A_ = nc.alloc_sbuf_tensor("arenaA", [P, ND * S], BF16)
    B_ = nc.alloc_sbuf_tensor("arenaB", [P, ND * S], BF16)
    W_ = nc.alloc_sbuf_tensor("arenaW", [P, 12288], BF16)
    SB_ = nc.alloc_sbuf_tensor("sb1", [P, 4096], BF16)
    XF_ = nc.alloc_sbuf_tensor("xf", [P, 2048], F32)
    ST_ = nc.alloc_sbuf_tensor("stt", [P, 6528], BF16)
    id_ = nc.alloc_sbuf_tensor("ident", [P, P], BF16)
    h128_ = nc.alloc_sbuf_tensor("h128", [P, P], BF16)
    mk_ = nc.alloc_sbuf_tensor("masks", [P, 2048], BF16)
    cs_ = nc.alloc_sbuf_tensor("cosb", [P, 512], F32)
    sn_ = nc.alloc_sbuf_tensor("sinb", [P, 512], F32)
    oc_ = nc.alloc_sbuf_tensor("onesc", [P, 1], BF16)
    or_ = nc.alloc_sbuf_tensor("onesr", [1, P], BF16)

    A = A_.ap()
    B = B_.ap()
    Wb = W_.ap()
    sb1 = SB_.ap()
    xf = XF_.ap()
    STb = ST_.ap()
    STf = STb.bitcast(F32)
    ident = id_.ap()
    h128 = h128_.ap()
    masks = mk_.ap()
    cosT = cs_.ap()
    sinT = sn_.ap()
    onesc = oc_.ap()
    onesr = or_.ap()

    Af32 = A.bitcast(F32)
    Bf32 = B.bitcast(F32)
    A3 = A.rearrange("p (c f) -> p c f", f=S)
    B3 = B.rearrange("p (c f) -> p c f", f=S)

    frow = STf[0:1, 160:2208]
    brow = STb[0:1, 4416:6464]

    PS_ = nc.alloc_psum_tensor("ps", [P, 4096], F32)
    PS = PS_.ap()

    sems = {}
    ctxs = []
    for ch in ("cs", "cg", "ct", "cv", "cb"):
        cm = nc.semaphore(ch)
        sems[ch] = cm.__enter__()
        ctxs.append(cm)
    blkcm = nc.Block()
    blk = blkcm.__enter__()

    engines = {"sync": nc.sync, "gpsimd": nc.gpsimd, "tensor": nc.tensor,
               "vector": nc.vector, "scalar": nc.scalar}

    RSQ = 1.0 / math.sqrt(128.0)

    def emit(em: Em):
        em.dma(ident, identf)
        em.dma(h128, h128f)
        em.dma(masks, masksf)
        em.dma(cosT, cosf)
        em.dma(sinT, sinf)
        em.dma(STf[0:1, 48:64], gain)
        em.fence("vector")
        em.v(lambda e: e.memset(onesc, 1.0))
        em.v(lambda e: e.memset(onesr, 1.0))
        em.v(lambda e: e.tensor_scalar(STf[0:1, 48:64], STf[0:1, 48:64],
                                       RSQ, None, ALU.mult))
        em.v(lambda e: e.tensor_scalar(STb[0:1, 4400:4416], STf[0:1, 48:64],
                                       1.0, None, ALU.mult))
        em.v(lambda e: e.tensor_tensor(STf[0:1, 100:116], STf[0:1, 48:64],
                                       STb[0:1, 4400:4416], ALU.subtract))
        em.v(lambda e: e.tensor_scalar(STb[0:1, 4384:4400], STf[0:1, 100:116],
                                       1.0, None, ALU.mult))
        em.fence("tensor")
        em.t(lambda e: e.matmul(PS[:, 0:16], onesr, STb[0:1, 4400:4416],
                                start=True, stop=False))
        em.t(lambda e: e.matmul(PS[:, 0:16], onesr, STb[0:1, 4384:4400],
                                start=False, stop=True))
        em.fence("scalar")
        em.s(lambda e: e.copy(STf[:, 120:136], PS[:, 0:16]))
        em.fence_all()

        def trans4(pairs):
            em.fence("tensor")
            for k, (_, src) in enumerate(pairs):
                em.t(lambda e, k=k, src=src: e.matmul(
                    PS[:, 3584 + k * 128:3712 + k * 128], src, ident,
                    start=True, stop=True))
            em.fence("scalar")
            for k, (dst, _) in enumerate(pairs):
                em.s(lambda e, k=k, dst=dst: e.copy(
                    dst, PS[:, 3584 + k * 128:3712 + k * 128]))

        def stage_w(dram, col0, width, nchunk=16):
            W3 = Wb.rearrange("p (c f) -> p c f", f=width)
            em.fence("gpsimd")
            for i in range(nchunk):
                em.dma(W3[:, i, :], dram[i * P:(i + 1) * P, col0:col0 + width],
                       q="gpsimd")
            return W3

        def ln_pass(src):
            for t in range(NT):
                em.fence("sync")
                em.dma(xf, src[t * P:(t + 1) * P, :])
                em.fence("vector", "tensor")
                x3 = xf.rearrange("p (a f) -> p a f", a=1)
                em.v(lambda e, x3=x3: e.reduce_sum(STf[:, 0:1], x3, axis=AX.X))
                em.v(lambda e: e.tensor_tensor(PS[:, 0:2048], xf, xf, ALU.mult))
                em.v(lambda e: e.reduce_sum(
                    STf[:, 1:2],
                    PS[:, 0:2048].rearrange("p (a f) -> p a f", a=1), axis=AX.X))
                em.v(lambda e: e.tensor_scalar(STf[:, 2:3], STf[:, 0:1],
                                               1.0 / D, None, ALU.mult))
                em.v(lambda e: e.tensor_tensor(STf[:, 3:4], STf[:, 2:3],
                                               STf[:, 2:3], ALU.mult))
                em.v(lambda e: e.tensor_scalar(STf[:, 4:5], STf[:, 1:2],
                                               1.0 / D, None, ALU.mult))
                em.v(lambda e: e.tensor_tensor(STf[:, 4:5], STf[:, 4:5],
                                               STf[:, 3:4], ALU.subtract))
                em.v(lambda e: e.tensor_scalar(STf[:, 4:5], STf[:, 4:5],
                                               1.0, 1e-5, ALU.mult, ALU.add))
                em.fence("scalar")
                em.s(lambda e: e.activation(STf[:, 5:6], STf[:, 4:5], AF.Sqrt))
                em.fence("vector")
                em.v(lambda e: e.reciprocal(STf[:, 6:7], STf[:, 5:6]))
                em.v(lambda e: e.tensor_scalar(
                    sb1[:, 0:2048], xf, STf[:, 2:3], STf[:, 6:7],
                    ALU.subtract, ALU.mult))
                for g in range(4):
                    trans4([(A3[:, g * 4 + k, t * P:(t + 1) * P],
                             sb1[:, (g * 4 + k) * P:(g * 4 + k + 1) * P])
                            for k in range(4)])

        def rms_rope(ps_off, width, nseg, do_rope, rope_off, t):
            src = PS[:, ps_off:ps_off + width]
            dst = PS[:, ps_off + width:ps_off + 2 * width]
            sxf = xf[:, 1024:1024 + width]
            em.v(lambda e: e.tensor_scalar(sxf, src, 1.0, None, ALU.mult))
            em.v(lambda e: e.tensor_tensor(dst, sxf, sxf, ALU.mult))
            em.v(lambda e: e.reduce_sum(
                STf[:, 8:8 + nseg],
                dst.rearrange("p (s d) -> p s d", d=64), axis=AX.X))
            em.v(lambda e: e.tensor_scalar(STf[:, 8:8 + nseg], STf[:, 8:8 + nseg],
                                           1.0 / 64, 1e-6, ALU.mult, ALU.add))
            em.fence("scalar")
            em.s(lambda e: e.activation(STf[:, 40:40 + nseg],
                                        STf[:, 8:8 + nseg], AF.Sqrt))
            em.fence("vector")
            em.v(lambda e: e.reciprocal(STf[:, 72:72 + nseg],
                                        STf[:, 40:40 + nseg]))
            st3 = STf[:, 72:72 + nseg].rearrange(
                "p (s o) -> p s o", o=1).broadcast_to((P, nseg, 64))
            em.v(lambda e: e.tensor_tensor(
                dst.rearrange("p (s d) -> p s d", d=64),
                sxf.rearrange("p (s d) -> p s d", d=64), st3, ALU.mult))
            if do_rope:
                nh = width // 128 if rope_off else width // 64
                span = 128 if rope_off else 64
                r3 = dst.rearrange("p (h q) -> p h q", q=span)
                x1 = r3[:, :, rope_off:rope_off + 32]
                x2 = r3[:, :, rope_off + 32:rope_off + 64]
                cc = cosT[:, t * 32:(t + 1) * 32].rearrange(
                    "p (o f) -> p o f", o=1).broadcast_to((P, nh, 32))
                ss = sinT[:, t * 32:(t + 1) * 32].rearrange(
                    "p (o f) -> p o f", o=1).broadcast_to((P, nh, 32))
                q1 = xf[:, 0:nh * 32].rearrange("p (h q) -> p h q", q=32)
                q2 = xf[:, 256:256 + nh * 32].rearrange("p (h q) -> p h q", q=32)
                q3 = xf[:, 512:512 + nh * 32].rearrange("p (h q) -> p h q", q=32)
                em.v(lambda e: e.tensor_tensor(q1, x1, cc, ALU.mult))
                em.v(lambda e: e.tensor_tensor(q2, x2, ss, ALU.mult))
                em.v(lambda e: e.tensor_tensor(q3, x1, ss, ALU.mult))
                em.v(lambda e: e.tensor_tensor(x1, q1, q2, ALU.add))
                em.v(lambda e: e.tensor_tensor(q1, x2, cc, ALU.mult))
                em.v(lambda e: e.tensor_tensor(x2, q1, q3, ALU.subtract))
            return dst

        w2k3 = w2k.rearrange("p (i d) -> p i d", d=D)
        w2p3 = w2p.rearrange("(j p) d -> p j d", p=P)
        for b in range(16):
            em.fence("sync")
            em.dma(Af32[:, 0:8192].rearrange("p (i d) -> p i d", d=128),
                   w2k3[:, :, b * P:(b + 1) * P])
            em.fence("vector")
            cur, oth = Af32, Bf32
            h = 1
            while h < 64:
                for g in range(64 // (2 * h)):
                    a0 = g * 2 * h * 128
                    w = h * 128
                    em.v(lambda e, a0=a0, w=w, cur=cur, oth=oth: e.tensor_tensor(
                        oth[:, a0:a0 + w], cur[:, a0:a0 + w],
                        cur[:, a0 + w:a0 + 2 * w], ALU.add))
                    em.v(lambda e, a0=a0, w=w, cur=cur, oth=oth: e.tensor_tensor(
                        oth[:, a0 + w:a0 + 2 * w], cur[:, a0:a0 + w],
                        cur[:, a0 + w:a0 + 2 * w], ALU.subtract))
                cur, oth = oth, cur
                h *= 2
            em.fence("vector")
            em.v(lambda e, cur=cur: e.tensor_scalar(
                Wb[:, 0:8192], cur[:, 0:8192], 1.0 / math.sqrt(HID),
                None, ALU.mult))
            for jq in range(16):
                em.fence("tensor")
                for k in range(4):
                    em.t(lambda e, jq=jq, k=k: e.matmul(
                        PS[:, k * 128:(k + 1) * 128], h128,
                        Wb[:, (jq * 4 + k) * 128:(jq * 4 + k + 1) * 128],
                        start=True, stop=True))
                em.fence("scalar")
                for k in range(4):
                    em.s(lambda e, k=k: e.copy(sb1[:, k * 128:(k + 1) * 128],
                                               PS[:, k * 128:(k + 1) * 128]))
                em.fence("sync")
                em.dma(w2p3[:, jq * 4:jq * 4 + 4, b * P:(b + 1) * P],
                       sb1[:, 0:512].rearrange("p (j f) -> p j f", f=128))
        em.fence_all()

        ln_pass(x_in)
        em.fence_all()
        if stop == "ln0":
            em.dma(adbg, A)
            em.fence_all()
            return
        if stop == "stw":
            stage_w(wqk, 0, 512)
            em.fence_all()
            em.dma(adbg[:, 0:8192], Wb[:, 0:8192])
            em.fence_all()
            return

        for c in range(ND):
            em.fence("sync")
            em.dma(xf, fw[c * P:(c + 1) * P, :])
            em.fence("vector")
            em.v(lambda e: e.scalar_tensor_tensor(PS[:, 0:2048], xf, -1.0, xf,
                                                  ALU.mult, ALU.max))
            em.v(lambda e: e.reduce_sum(
                STf[:, 0:1],
                PS[:, 0:2048].rearrange("p (a f) -> p a f", a=1), axis=AX.X))
            em.v(lambda e: e.tensor_scalar(STf[:, 1:2], STf[:, 0:1],
                                           0.7 / D, None, ALU.mult))
            em.v(lambda e: e.tensor_scalar(STf[:, 2:3], STf[:, 1:2],
                                           -1.0, None, ALU.mult))
            em.v(lambda e: e.tensor_scalar(sb1[:, 0:2048], xf,
                                           STf[:, 1:2], None, ALU.is_gt))
            em.v(lambda e: e.tensor_scalar(PS[:, 2048:4096], xf,
                                           STf[:, 2:3], None, ALU.is_lt))
            em.v(lambda e: e.tensor_tensor(sb1[:, 2048:4096], sb1[:, 0:2048],
                                           PS[:, 2048:4096], ALU.subtract))
            for g in range(4):
                trans4([(B3[:, g * 4 + k, c * P:(c + 1) * P],
                         sb1[:, 2048 + (g * 4 + k) * P:2048 + (g * 4 + k + 1) * P])
                        for k in range(4)])
        em.fence("sync")
        em.dma(frow, fb)
        em.fence("vector")
        em.v(lambda e: e.tensor_scalar(brow, frow, 1.0, None, ALU.mult))
        em.fence_all()
        for t in range(NT):
            em.fence("tensor", "sync", "gpsimd")
            em.dma(xf, x_in[t * P:(t + 1) * P, :], q="gpsimd")
            for j in range(4):
                for i in range(ND):
                    em.t(lambda e, i=i, j=j, t=t: e.matmul(
                        PS[:, j * 512:(j + 1) * 512], A3[:, i, t * P:(t + 1) * P],
                        B3[:, i, j * 512:(j + 1) * 512],
                        start=(i == 0), stop=False))
                em.t(lambda e, j=j: e.matmul(
                    PS[:, j * 512:(j + 1) * 512], onesr,
                    brow[0:1, j * 512:(j + 1) * 512], start=False, stop=True))
            em.fence("vector")
            em.v(lambda e: e.tensor_tensor(xf, PS[:, 0:2048], xf, ALU.add))
            em.fence("sync")
            em.dma(acc_hbm[t * P:(t + 1) * P, :], xf)
        em.fence_all()
        if stop == "ffn0":
            em.dma(adbg, B)
            em.fence_all()
            return

        for pc in range(6):
            W3 = stage_w(wqk, pc * 512, 512)
            em.fence_all()
            for t in range(NT):
                em.fence("tensor")
                for i in range(ND):
                    em.t(lambda e, i=i, t=t: e.matmul(
                        PS[:, 0:512], A3[:, i, t * P:(t + 1) * P],
                        W3[:, i, :], start=(i == 0), stop=(i == ND - 1)))
                em.fence("vector")
                if pc < 4:
                    rms_rope(0, 512, 8, True, 64, t)
                    em.v(lambda e: e.tensor_scalar(sb1[:, 0:512], PS[:, 512:1024],
                                                   1.0, None, ALU.mult))
                    trans4([(sb1[:, 1024 + k * 128:1152 + k * 128],
                             sb1[:, k * 128:(k + 1) * 128]) for k in range(4)])
                    em.fence("sync")
                    for k in range(4):
                        em.dma(qT_hbm[pc * 4 + k, :, t * P:(t + 1) * P],
                               sb1[:, 1024 + k * 128:1152 + k * 128])
                else:
                    rms_rope(0, 512, 8, True, 0, t)
                    em.v(lambda e: e.tensor_scalar(sb1[:, 0:512], PS[:, 512:1024],
                                                   1.0, None, ALU.mult))
                    em.fence("sync")
                    em.dma(kr_hbm[t * P:(t + 1) * P, (pc - 4) * 512:(pc - 3) * 512],
                           sb1[:, 0:512])
        em.fence_all()
        if stop == "qkrope":
            return

        W3 = stage_w(wkvd, 0, 512)
        em.fence_all()
        B3c = B.rearrange("p (c f) -> p c f", f=S)
        for t in range(NT):
            em.fence("tensor")
            for i in range(ND):
                em.t(lambda e, i=i, t=t: e.matmul(
                    PS[:, 0:512], A3[:, i, t * P:(t + 1) * P],
                    W3[:, i, :], start=(i == 0), stop=(i == ND - 1)))
            em.v(lambda e: e.tensor_scalar(xf[:, 1024:1536], PS[:, 0:512],
                                           1.0, None, ALU.mult))
            em.v(lambda e: e.tensor_tensor(PS[:, 512:1024], xf[:, 1024:1536],
                                           xf[:, 1024:1536], ALU.mult))
            em.v(lambda e: e.reduce_sum(
                STf[:, 8:9],
                PS[:, 512:1024].rearrange("p (a f) -> p a f", a=1), axis=AX.X))
            em.v(lambda e: e.tensor_scalar(STf[:, 8:9], STf[:, 8:9],
                                           1.0 / 512, 1e-6, ALU.mult, ALU.add))
            em.fence("scalar")
            em.s(lambda e: e.activation(STf[:, 9:10], STf[:, 8:9], AF.Sqrt))
            em.fence("vector")
            em.v(lambda e: e.reciprocal(STf[:, 10:11], STf[:, 9:10]))
            em.v(lambda e: e.tensor_scalar(sb1[:, 0:512], xf[:, 1024:1536],
                                           STf[:, 10:11], None, ALU.mult))
            trans4([(B3c[:, k, t * P:(t + 1) * P],
                     sb1[:, k * 128:(k + 1) * 128]) for k in range(4)])
        em.fence_all()

        W3u = stage_w(wkvu, 0, 3072, nchunk=4)
        em.fence_all()
        for t in range(NT):
            em.fence("tensor", "sync", "gpsimd")
            em.dma(sb1[:, 1024:2048], kr_hbm[t * P:(t + 1) * P, :], q="gpsimd")
            for nb in range(6):
                for i in range(4):
                    em.t(lambda e, i=i, nb=nb, t=t: e.matmul(
                        PS[:, nb * 512:(nb + 1) * 512],
                        B3c[:, i, t * P:(t + 1) * P],
                        W3u[:, i, nb * 512:(nb + 1) * 512],
                        start=(i == 0), stop=(i == 3)))
            em.fence("vector", "scalar")
            em.v(lambda e: e.tensor_scalar(xf[:, 1024:2048], PS[:, 0:1024],
                                           1.0, None, ALU.mult))
            em.v(lambda e: e.tensor_tensor(PS[:, 3072:4096], xf[:, 1024:2048],
                                           xf[:, 1024:2048], ALU.mult))
            em.v(lambda e: e.reduce_sum(
                STf[:, 8:24],
                PS[:, 3072:4096].rearrange("p (s d) -> p s d", d=64), axis=AX.X))
            em.v(lambda e: e.tensor_scalar(STf[:, 8:24], STf[:, 8:24],
                                           1.0 / 64, 1e-6, ALU.mult, ALU.add))
            em.fence("scalar")
            em.s(lambda e: e.activation(STf[:, 40:56], STf[:, 8:24], AF.Sqrt))
            em.fence("vector")
            em.v(lambda e: e.reciprocal(STf[:, 72:88], STf[:, 40:56]))
            st3 = STf[:, 72:88].rearrange("p (s o) -> p s o", o=1).broadcast_to((P, 16, 64))
            em.v(lambda e: e.tensor_tensor(
                sb1[:, 0:1024].rearrange("p (s d) -> p s d", d=64),
                xf[:, 1024:2048].rearrange("p (s d) -> p s d", d=64),
                st3, ALU.mult))
            kf3 = sb1[:, 2048:4096].rearrange("p (h q) -> p h q", q=128)
            em.v(lambda e: e.tensor_scalar(
                kf3[:, :, 0:64],
                sb1[:, 0:1024].rearrange("p (s d) -> p s d", d=64),
                1.0, None, ALU.mult))
            em.v(lambda e: e.tensor_scalar(
                kf3[:, :, 64:128],
                sb1[:, 1024:2048].rearrange("p (s d) -> p s d", d=64),
                1.0, None, ALU.mult))
            for g in range(4):
                trans4([(A3[:, g * 4 + k, t * P:(t + 1) * P],
                         sb1[:, 2048 + (g * 4 + k) * P:2048 + (g * 4 + k + 1) * P])
                        for k in range(4)])
            em.fence("vector")
            em.v(lambda e: e.tensor_scalar(sb1[:, 0:2048], PS[:, 1024:3072],
                                           1.0, None, ALU.mult))
            em.fence("sync")
            em.dma(v_hbm[t * P:(t + 1) * P, :], sb1[:, 0:2048])
        em.fence_all()

        v_hbm3 = v_hbm.rearrange("(u p) d -> p u d", p=P)

        def attention(kT3, layer):
            for h in range(H):
                em.fence("sync", "gpsimd")
                em.dma(Wb[:, 0:2048], qT_hbm[h])
                em.dma(Wb[:, 2048:4096].rearrange("p (u f) -> p u f", f=128),
                       v_hbm3[:, :, h * 128:(h + 1) * 128], q="gpsimd")
                em.fence_all()
                for qb in range(4):
                    nu = 4 * qb + 4
                    for u in range(nu):
                        em.t(lambda e, u=u, h=h, qb=qb: e.matmul(
                            PS[:, 0:512], kT3[:, h, u * P:(u + 1) * P],
                            Wb[:, qb * 512:(qb + 1) * 512],
                            start=True, stop=True))
                        em.fence("scalar")
                        if layer == 0:
                            em.s(lambda e, h=h: e.activation(
                                sb1[:, 0:512], PS[:, 0:512], AF.Exp,
                                scale=STf[:, 120 + h:121 + h]))
                        else:
                            em.s(lambda e: e.activation(
                                sb1[:, 0:512], PS[:, 0:512], AF.Exp, scale=RSQ))
                        j = u - 4 * qb
                        em.fence("vector")
                        if j >= 0:
                            em.v(lambda e, j=j: e.tensor_tensor(
                                sb1[:, 0:512], sb1[:, 0:512],
                                masks[:, j * 512:(j + 1) * 512], ALU.mult))
                        em.fence("tensor")
                        em.t(lambda e, u=u: e.matmul(
                            PS[:, 1024:1536], Wb[:, 2048 + u * 128:2176 + u * 128],
                            sb1[:, 0:512], start=(u == 0), stop=(u == nu - 1)))
                        em.t(lambda e, u=u: e.matmul(
                            PS[0:1, 1536:2048], onesc,
                            sb1[:, 0:512], start=(u == 0), stop=(u == nu - 1)))
                    em.fence("vector", "scalar")
                    em.v(lambda e: e.reciprocal(xf[0:1, 1024:1536],
                                                PS[0:1, 1536:2048]))
                    em.v(lambda e: e.tensor_scalar(sb1[0:1, 3072:3584],
                                                   xf[0:1, 1024:1536],
                                                   1.0, None, ALU.mult))
                    em.v(lambda e: e.tensor_tensor(sb1[0:1, 3584:4096],
                                                   xf[0:1, 1024:1536],
                                                   sb1[0:1, 3072:3584],
                                                   ALU.subtract))
                    em.s(lambda e: e.copy(xf[:, 0:512], PS[:, 1024:1536]))
                    em.fence("tensor")
                    em.t(lambda e: e.matmul(PS[:, 2048:2560], onesr,
                                            sb1[0:1, 3072:3584],
                                            start=True, stop=False))
                    em.t(lambda e: e.matmul(PS[:, 2048:2560], onesr,
                                            sb1[0:1, 3584:4096],
                                            start=False, stop=True))
                    em.fence("vector")
                    if layer == 0:
                        em.v(lambda e, h=h, qb=qb: e.tensor_tensor(
                            B3[:, h, qb * 512:(qb + 1) * 512],
                            xf[:, 0:512], PS[:, 2048:2560], ALU.mult))
                    else:
                        em.v(lambda e: e.tensor_tensor(
                            sb1[:, 1024:1536],
                            xf[:, 0:512], PS[:, 2048:2560], ALU.mult))
                        em.fence("sync")
                        em.dma(yT_hbm[h, :, qb * 512:(qb + 1) * 512],
                               sb1[:, 1024:1536])

        if stop == "kvup":
            em.dma(adbg, B)
            em.fence_all()
            return
        attention(A3, 0)
        em.fence_all()
        if stop == "attn0":
            return

        def proj_pass(yT3, wsrc, bias_row):
            for q in range(4):
                W3 = stage_w(wsrc, q * 512, 512)
                em.fence_all()
                for t in range(NT):
                    em.fence("tensor", "sync", "gpsimd")
                    em.dma(xf[:, 0:512],
                           acc_hbm[t * P:(t + 1) * P, q * 512:(q + 1) * 512],
                           q="gpsimd")
                    for i in range(ND):
                        em.t(lambda e, i=i, t=t, yT3=yT3, W3=W3: e.matmul(
                            PS[:, 0:512], yT3[:, i, t * P:(t + 1) * P],
                            W3[:, i, :], start=(i == 0), stop=(bias_row is None and i == ND - 1)))
                    if bias_row is not None:
                        em.t(lambda e, q=q: e.matmul(
                            PS[:, 0:512], onesr,
                            bias_row[0:1, q * 512:(q + 1) * 512],
                            start=False, stop=True))
                    em.fence("vector", "gpsimd")
                    em.v(lambda e: e.tensor_tensor(xf[:, 0:512], PS[:, 0:512],
                                                   xf[:, 0:512], ALU.add))
                    em.fence("sync")
                    em.dma(acc_hbm[t * P:(t + 1) * P, q * 512:(q + 1) * 512],
                           xf[:, 0:512])

        proj_pass(B3, wproj, None)
        em.fence_all()
        if stop == "proj0":
            return

        ln_pass(acc_hbm)
        em.fence_all()

        for wsrc, bsrc, kind in ((wq1, bq1, "q"), (wk1, bk1, "k"),
                                 (wv1, bv1, "v")):
            em.fence("sync")
            em.dma(frow, bsrc)
            em.fence("vector")
            em.v(lambda e: e.tensor_scalar(brow, frow, 1.0, None, ALU.mult))
            for q in range(4):
                W3 = stage_w(wsrc, q * 512, 512)
                em.fence_all()
                for t in range(NT):
                    em.fence("tensor")
                    for i in range(ND):
                        em.t(lambda e, i=i, t=t, W3=W3: e.matmul(
                            PS[:, 0:512], A3[:, i, t * P:(t + 1) * P],
                            W3[:, i, :], start=(i == 0), stop=False))
                    em.t(lambda e, q=q: e.matmul(
                        PS[:, 0:512], onesr, brow[0:1, q * 512:(q + 1) * 512],
                        start=False, stop=True))
                    em.fence("vector")
                    em.v(lambda e: e.tensor_scalar(sb1[:, 0:512], PS[:, 0:512],
                                                   1.0, None, ALU.mult))
                    if kind == "v":
                        em.fence("sync")
                        em.dma(v_hbm[t * P:(t + 1) * P, q * 512:(q + 1) * 512],
                               sb1[:, 0:512])
                    elif kind == "k":
                        trans4([(B3[:, q * 4 + k, t * P:(t + 1) * P],
                                 sb1[:, k * 128:(k + 1) * 128])
                                for k in range(4)])
                    else:
                        trans4([(sb1[:, 1024 + k * 128:1152 + k * 128],
                                 sb1[:, k * 128:(k + 1) * 128])
                                for k in range(4)])
                        em.fence("sync")
                        for k in range(4):
                            em.dma(qT_hbm[q * 4 + k, :, t * P:(t + 1) * P],
                                   sb1[:, 1024 + k * 128:1152 + k * 128])
        em.fence_all()

        attention(B3, 1)
        em.fence_all()
        if stop == "attn1":
            return

        em.fence("sync")
        for h in range(H):
            em.dma(B3[:, h, :], yT_hbm[h])
        em.fence("sync")
        em.dma(frow, bo1)
        em.fence("vector")
        em.v(lambda e: e.tensor_scalar(brow, frow, 1.0, None, ALU.mult))
        em.fence_all()
        proj_pass(B3, wo1, brow)
        em.fence_all()
        if stop == "proj1":
            return

        B3h = B.rearrange("p (c f) -> p c f", f=S)
        B3w3 = B[:, 16384:24576].rearrange("p (c f) -> p c f", f=512)
        for g in range(8):
            for sub in range(2):
                col0 = g * 1024 + sub * 512
                W3a = stage_w(w1t, col0, 512)
                em.fence("gpsimd")
                for i in range(ND):
                    em.dma(B3w3[:, i, :], w3t[i * P:(i + 1) * P, col0:col0 + 512],
                           q="gpsimd")
                em.fence_all()
                for t in range(NT):
                    em.fence("tensor")
                    for i in range(ND):
                        em.t(lambda e, i=i, t=t, W3a=W3a: e.matmul(
                            PS[:, 0:512], A3[:, i, t * P:(t + 1) * P],
                            W3a[:, i, :], start=(i == 0), stop=(i == ND - 1)))
                    for i in range(ND):
                        em.t(lambda e, i=i, t=t: e.matmul(
                            PS[:, 512:1024], A3[:, i, t * P:(t + 1) * P],
                            B3w3[:, i, :], start=(i == 0), stop=(i == ND - 1)))
                    em.fence("scalar")
                    em.s(lambda e: e.activation(xf[:, 0:512], PS[:, 0:512],
                                                AF.Silu))
                    em.fence("vector")
                    em.v(lambda e: e.tensor_tensor(sb1[:, 0:512], xf[:, 0:512],
                                                   PS[:, 512:1024], ALU.mult))
                    trans4([(B3h[:, sub * 4 + k, t * P:(t + 1) * P],
                             sb1[:, k * 128:(k + 1) * 128]) for k in range(4)])
            for dh in range(2):
                W3b = Wb.rearrange("p (c f) -> p c f", f=1024)
                em.fence("sync")
                for k in range(8):
                    em.dma(W3b[:, k, :],
                           w2p[(g * 8 + k) * P:(g * 8 + k + 1) * P,
                               dh * 1024:(dh + 1) * 1024])
                em.fence_all()
                for t in range(NT):
                    em.fence("tensor", "sync", "gpsimd")
                    em.dma(xf[:, 0:1024],
                           acc_hbm[t * P:(t + 1) * P, dh * 1024:(dh + 1) * 1024],
                           q="gpsimd")
                    for k in range(8):
                        for nb in range(2):
                            em.t(lambda e, k=k, nb=nb, t=t, W3b=W3b: e.matmul(
                                PS[:, nb * 512:(nb + 1) * 512],
                                B3h[:, k, t * P:(t + 1) * P],
                                W3b[:, k, nb * 512:(nb + 1) * 512],
                                start=(k == 0), stop=(k == 7)))
                    em.fence("vector", "gpsimd")
                    em.v(lambda e: e.tensor_tensor(xf[:, 0:1024], PS[:, 0:1024],
                                                   xf[:, 0:1024], ALU.add))
                    em.fence("sync")
                    dst = out if g == 7 else acc_hbm
                    em.dma(dst[t * P:(t + 1) * P, dh * 1024:(dh + 1) * 1024],
                           xf[:, 0:1024])
        em.fence_all()

    for name in ENGINES:
        getattr(blk, name)(lambda eng, name=name: emit(
            Em(name, engines, sems, paranoid=paranoid, selfwait=selfwait)))

    blkcm.__exit__(None, None, None)
    for cm in reversed(ctxs):
        cm.__exit__(None, None, None)
    return nc


def _had(n):
    h = np.array([[1.0]], dtype=np.float32)
    while h.shape[0] < n:
        h = np.block([[h, h], [h, -h]])
    return h


def _consts():
    bf = ml_dtypes.bfloat16
    identf = np.eye(P, dtype=np.float32).astype(bf)
    h128f = _had(P).astype(bf)
    masks = np.zeros((P, 4, 512), np.float32)
    r = np.arange(P)[:, None]
    c = np.arange(512)[None, :]
    for j in range(4):
        masks[:, j, :] = (c >= j * 128 + r).astype(np.float32)
    masksf = masks.reshape(P, 2048).astype(bf)
    inv = 10000.0 ** (-np.arange(0, 64, 2, dtype=np.float32) / 64.0)
    tpos = np.arange(S, dtype=np.float32)
    fr = np.outer(tpos, inv)
    cosf = np.cos(fr).reshape(NT, P, 32).transpose(1, 0, 2).reshape(P, 512)
    sinf = np.sin(fr).reshape(NT, P, 32).transpose(1, 0, 2).reshape(P, 512)
    return identf, h128f, masksf, cosf.astype(np.float32), sinf.astype(np.float32)


_NC = None


def kernel(L0_attn_x, L0_attn_w_q_krope, L0_attn_w_kv_down, L0_attn_w_kv_up,
           L0_attn_w_proj, L0_attn_q_gain, L0_ffn_weight, L0_ffn_bias,
           L1_attn_Wq, L1_attn_bq, L1_attn_Wk, L1_attn_bk, L1_attn_Wv,
           L1_attn_bv, L1_attn_Wo, L1_attn_bo, L1_ffn_W1, L1_ffn_W2,
           L1_ffn_W3):
    global _NC
    if _NC is None:
        _NC = build(paranoid=False, selfwait=True)
    nc = _NC
    f32 = np.float32
    cc = np.ascontiguousarray
    identf, h128f, masksf, cosf, sinf = _consts()
    w2k = cc(np.asarray(L1_ffn_W2, f32).T.reshape(64, P, D)
             .transpose(1, 0, 2).reshape(P, 64 * D))
    shared = {
        "wqk": cc(np.asarray(L0_attn_w_q_krope, f32)),
        "wkvd": cc(np.asarray(L0_attn_w_kv_down, f32)),
        "wkvu": cc(np.asarray(L0_attn_w_kv_up, f32)),
        "wproj": cc(np.asarray(L0_attn_w_proj, f32)),
        "gain": cc(np.asarray(L0_attn_q_gain, f32).reshape(1, H)),
        "fw": cc(np.asarray(L0_ffn_weight, f32)),
        "fb": cc(np.asarray(L0_ffn_bias, f32).reshape(1, D)),
        "wq1": cc(np.asarray(L1_attn_Wq, f32).T),
        "wk1": cc(np.asarray(L1_attn_Wk, f32).T),
        "wv1": cc(np.asarray(L1_attn_Wv, f32).T),
        "wo1": cc(np.asarray(L1_attn_Wo, f32).T),
        "bq1": cc(np.asarray(L1_attn_bq, f32).reshape(1, D)),
        "bk1": cc(np.asarray(L1_attn_bk, f32).reshape(1, D)),
        "bv1": cc(np.asarray(L1_attn_bv, f32).reshape(1, D)),
        "bo1": cc(np.asarray(L1_attn_bo, f32).reshape(1, D)),
        "w1t": cc(np.asarray(L1_ffn_W1, f32).T),
        "w3t": cc(np.asarray(L1_ffn_W3, f32).T),
        "w2k": w2k,
        "identf": identf, "h128f": h128f, "masksf": masksf,
        "cosf": cosf, "sinf": sinf,
    }
    x = np.asarray(L0_attn_x, f32)
    in_maps = []
    for b in range(8):
        m = dict(shared)
        m["x"] = cc(x[b])
        in_maps.append(m)
    res = run_bass_kernel_spmd(nc, in_maps, core_ids=list(range(8)))
    return np.stack([r["out"] for r in res.results], axis=0).astype(np.float32)
